# version 01 (final)
# baseline (speedup 1.0000x reference)
import math
import os
from contextlib import ExitStack

import numpy as np
import ml_dtypes

import concourse.bass as bass
import concourse.mybir as mybir
from concourse.bass_utils import run_bass_kernel_spmd

F32 = mybir.dt.float32
BF16 = mybir.dt.bfloat16
AF = mybir.ActivationFunctionType
ALU = mybir.AluOpType
AX = mybir.AxisListType.X

D = 4096
NH = 32
HD = 128
T = 2048
S = 4096
ESL = 512
HTOK = 1024
HTILES = 8
A_CUT = 50.0
EPS = 1e-5
SCALE = float(HD) ** -0.5
KSTOP = os.environ.get("KSTOP", "full")

SLOPES = [2.0 ** (-(h + 1) / 4.0) for h in range(NH)]


def _block_lists():
    A, B = [], []
    for p in range(16):
        lo, hi = 2 * p, 2 * p + 1
        if p % 2 == 0:
            A.append(hi); B.append(lo)
        else:
            A.append(lo); B.append(hi)
    return A, B


BLK_A, BLK_B = _block_lists()


def _alo(p, h):
    nb = int(math.ceil((A_CUT / SLOPES[h]) / 128.0)) + 1
    g = 2 * p + 1 - nb
    if g < 0:
        g = 0
    return (g & ~1) // 2


class Sem:
    __slots__ = ("name", "n", "h")

    def __init__(self, name):
        self.name, self.n, self.h = name, 0, None


class Plan:
    ENGS = ("sync", "scalar", "vector", "tensor", "gpsimd")

    def __init__(self):
        self.q = {e: [] for e in self.ENGS}
        self.sems = []

    def sem(self, name):
        s = Sem(name)
        self.sems.append(s)
        return s

    def emit(self, eng, fn):
        self.q[eng].append(fn)

    def wait(self, eng, sem, val):
        if val <= 0:
            return
        self.q[eng].append(lambda e, s=sem, v=val: e.wait_ge(s.h, v))

    def waits(self, eng, lst):
        best = {}
        for (s, v) in lst:
            if v > best.get(s, 0):
                best[s] = v
        for s, v in best.items():
            self.wait(eng, s, v)


def build_program():
    nc = bass.Bass("TRN2", target_bir_lowering=False, debug=False)

    x_in = nc.dram_tensor("x", [T, D], F32, kind="ExternalInput").ap()
    wqkvT = nc.dram_tensor("wqkvT", [D, 3 * D], F32, kind="ExternalInput").ap()
    woutT = nc.dram_tensor("woutT", [D, D], F32, kind="ExternalInput").ap()
    wbaseT = nc.dram_tensor("wbaseT", [D, D], F32, kind="ExternalInput").ap()
    smallT = nc.dram_tensor("smallT", [D, 144], F32, kind="ExternalInput").ap()
    bcatT = nc.dram_tensor("bcatT", [128, D], F32, kind="ExternalInput").ap()
    rows_in = nc.dram_tensor("rows", [6, D], F32, kind="ExternalInput").ap()
    maskT_in = nc.dram_tensor("maskT", [128, 4096], BF16, kind="ExternalInput").ap()
    alrows_in = nc.dram_tensor("alrows", [16, S], BF16, kind="ExternalInput").ap()
    ident_in = nc.dram_tensor("ident", [128, 128], BF16, kind="ExternalInput").ap()
    out = nc.dram_tensor("out", [T, D], F32, kind="ExternalOutput").ap()

    x1b = nc.dram_tensor("x1b", [T, D], F32, kind="Internal").ap()
    x2b = nc.dram_tensor("x2b", [T, D], F32, kind="Internal").ap()
    xnb = nc.dram_tensor("xnb", [T, D], BF16, kind="Internal").ap()
    krawb = nc.dram_tensor("krawb", [T, D], BF16, kind="Internal").ap()
    qrawb = nc.dram_tensor("qrawb", [T, D], BF16, kind="Internal").ap()
    qnb = nc.dram_tensor("qnb", [T, D], BF16, kind="Internal").ap()
    kvloc = nc.dram_tensor("kvloc", [T, 2 * D], BF16, kind="Internal").ap()
    kvg = nc.dram_tensor("kvg", [S, 2 * D], BF16, kind="Internal").ap()
    bbch = nc.dram_tensor("bbch", [128, D], F32, kind="Internal").ap()
    lnwbh = nc.dram_tensor("lnwbh", [4 * 128, D], BF16, kind="Internal").ap()

    plan = Plan()
    sLD = plan.sem("sLD")
    sWD = plan.sem("sWD")
    sST = plan.sem("sST")
    sT = plan.sem("sT")
    sV = plan.sem("sV")
    sA = plan.sem("sA")
    sP = plan.sem("sP")
    sCC = plan.sem("sCC")
    BARSEMS = None

    with ExitStack() as ctx:
        ARBF = ctx.enter_context(nc.sbuf_tensor("arbf", [128, 74752], BF16))
        ARF = ctx.enter_context(nc.sbuf_tensor("arf", [128, 12552], F32))
        ROWB = ctx.enter_context(nc.sbuf_tensor("rowb", [128, 4224], BF16))
        ROWF = ctx.enter_context(nc.sbuf_tensor("rowf", [128, 1024], F32))
        IDENT = ctx.enter_context(nc.sbuf_tensor("ident_sb", [128, 128], BF16))

        PB = [ctx.enter_context(nc.psum_tensor(f"pb{i}", [128, 512], F32))
              for i in range(6)]
        PX = [ctx.enter_context(nc.psum_tensor(f"px{i}", [128, 1024], BF16))
              for i in range(2)]

        for s in plan.sems:
            s.h = ctx.enter_context(nc.semaphore(s.name))
        BARSEMS = (sLD, sWD, sST, sT, sV, sA, sP)
        _names = ["XNF", "LNX0", "LNX1", "RES0", "RES1", "RES2", "RES3",
                  "BB0", "BB1", "ROW0", "ROW1", "KF0", "KF1", "VA0", "VA1",
                  "QF0", "QF1", "MISC", "RWST", "APK0", "APK1", "APLW",
                  "W0", "W1", "SMT", "BCS0", "BCS1", "DBG",
                  "GK0", "GK1", "GX0", "GX1", "GA0", "GA1", "GL0", "GL1",
                  "GS0", "GS1", "GBC"]
        SM = {}
        for _nm in _names:
            SM[_nm] = plan.sem(_nm)
            SM[_nm].h = ctx.enter_context(nc.semaphore(_nm))

        def bfv(off, n):
            return ARBF[:, off:off + n]

        def fv(off, n):
            return ARF[:, off:off + n]

        XNF = 0
        WSL = 32768
        LOC = 65536
        KSTG = 73728
        AKF = 0
        AVA = 8192
        AQF = 16512
        APB = 24704
        APT = 32896
        ASTG = 41088
        AMSK = 45184
        APLK = 0
        APLO = 4096
        APLW = 8192

        XF = 0
        XSTG = 8192
        FLOC = 9216
        STAT = 11328
        MURC = 11840
        SMALL = 12096
        assert SMALL + 456 <= 12552

        ALROW = [ROWB[0:1, 0:4096], ROWB[32:33, 0:4096]]
        SLCOL = [ROWB[0:1, 4096:4224], ROWB[32:33, 4096:4224]]
        SLBOTH = ROWB[0:64, 4096:4224]
        ROWST = ROWF[0:1, 0:512]
        ONESF = ROWF[0:1, 512:640]
        OHALF = ROWF[0:1, 640:768]
        OSCL = ROWF[0:1, 768:896]
        BTHRC = fv(SMALL + 448, 1)
        EPSC = fv(SMALL + 449, 1)

        def dma(eng, out_ap, in_ap, sem, waits=(), transpose=False):
            plan.waits(eng, [w for w in waits if w])
            sem.n += 16
            if transpose:
                plan.emit(eng, lambda e, o=out_ap, i=in_ap, s=sem:
                          e.dma_start_transpose(out=o, in_=i).then_inc(s.h, 16))
            else:
                plan.emit(eng, lambda e, o=out_ap, i=in_ap, s=sem:
                          e.dma_start(out=o, in_=i).then_inc(s.h, 16))
            return (sem, sem.n)

        def barrier():
            for eng in Plan.ENGS:
                for s in plan.sems:
                    if s is sCC:
                        continue
                    plan.wait(eng, s, s.n)

        def vop(fn):
            sV.n += 1
            plan.emit("vector", lambda e, f=fn, s=sV: f(e).then_inc(s.h, 1))
            return sV.n

        def vchain(fn):
            v = vop(fn)
            plan.wait("vector", sV, v)
            return v

        def aop(fn):
            sA.n += 1
            plan.emit("scalar", lambda e, f=fn, s=sA: f(e).then_inc(s.h, 1))
            return sA.n

        dma("sync", IDENT[:, :], ident_in[:, :], SM["MISC"])
        for ap_, val in ((ONESF, 1.0), (OHALF, 0.5), (OSCL, SCALE)):
            plan.emit("vector", lambda e, a=ap_, v=val: e.memset(a, v))
        plan.emit("vector", lambda e: e.memset(EPSC, EPS))
        v_ones = vop(lambda e: e.memset(fv(SMALL, 1), 0.0))

        def build_bcast(row_idx, sc_row, dst_hbm, dst_bf16):
            for j in range(D // 512):
                w = dma("sync", ROWST,
                        rows_in[row_idx:row_idx + 1, j * 512:(j + 1) * 512],
                        SM["RWST"], waits=[(sT, sT.n)])
                plan.waits("tensor", [w])
                plan.wait("tensor", sV, max(v_ones, sV.n))
                sT.n += 1
                plan.emit("tensor", lambda e, s=sT, scr=sc_row:
                          e.matmul(PB[5][:, 0:512], scr, ROWST,
                                   start=True, stop=True).then_inc(s.h, 1))
                plan.wait("vector", sT, sT.n)
                plan.wait("vector", SM["GBC"], SM["GBC"].n)
                stg = bfv(KSTG, 512) if dst_bf16 else fv(XSTG, 512)
                v = vop(lambda e, o=stg: e.tensor_copy(o, PB[5][:, 0:512]))
                dma("scalar", dst_hbm[:, j * 512:(j + 1) * 512], stg, SM["GBC"],
                    waits=[(sV, v)])

        build_bcast(4, OHALF, bbch, False)
        build_bcast(0, OSCL, lnwbh[0 * 128:1 * 128, :], True)
        build_bcast(1, OSCL, lnwbh[1 * 128:2 * 128, :], True)
        build_bcast(2, ONESF, lnwbh[2 * 128:3 * 128, :], True)
        build_bcast(3, ONESF, lnwbh[3 * 128:4 * 128, :], True)
        w = dma("sync", ROWF[0:1, 0:1], rows_in[5:6, 0:1], SM["RWST"],
                waits=[(sT, sT.n)])
        plan.waits("tensor", [w])
        plan.wait("tensor", sV, sV.n)
        sT.n += 1
        plan.emit("tensor", lambda e, s=sT:
                  e.matmul(PB[5][:, 0:1], ONESF, ROWF[0:1, 0:1],
                           start=True, stop=True).then_inc(s.h, 1))
        plan.wait("vector", sT, sT.n)
        vop(lambda e: e.tensor_copy(BTHRC, PB[5][:, 0:1]))
        barrier()

        def ln_pass(src, dst):
            last_store = [None, None]
            last_stats = [0, 0, 0, 0]
            for t in range(16):
                par = t % 2
                xt = fv(XF + par * 4096, 4096)
                waits = [last_store[par]] if last_store[par] else []
                wv = dma("sync", xt, src[t * 128:(t + 1) * 128, :],
                         SM[f"LNX{par}"], waits=waits)
                c0 = STAT + (t % 4) * 8
                if last_stats[t % 4]:
                    plan.wait("vector", sV, last_stats[t % 4])
                plan.waits("vector", [wv])
                v_sum = vop(lambda e, x=xt, c=c0:
                            e.reduce_sum(fv(c, 1), x, axis=AX))
                plan.waits("scalar", [wv])
                if last_store[par]:
                    plan.waits("scalar", [last_store[par]])
                thr = bfv(LOC + par * 4096, 4096)
                a_sq = aop(lambda e, x=xt, o=thr, c=c0:
                           e.activation(o, x, AF.Square,
                                        accum_out=fv(c + 1, 1)))
                plan.wait("vector", sV, v_sum)
                plan.wait("vector", sA, a_sq)
                vchain(lambda e, c=c0: e.tensor_single_scalar(
                    fv(c + 2, 1), fv(c, 1), 1.0 / D, ALU.mult))
                vchain(lambda e, c=c0: e.tensor_single_scalar(
                    fv(c + 3, 1), fv(c + 1, 1), 1.0 / D, ALU.mult))
                vchain(lambda e, c=c0: e.tensor_mul(
                    fv(c + 4, 1), fv(c + 2, 1), fv(c + 2, 1)))
                v_var = vchain(lambda e, c=c0: e.tensor_sub(
                    fv(c + 5, 1), fv(c + 3, 1), fv(c + 4, 1)))
                plan.wait("scalar", sV, v_var)
                a_rs = aop(lambda e, c=c0: e.activation(
                    fv(c + 6, 1), fv(c + 5, 1), AF.Sqrt, bias=EPSC))
                plan.wait("vector", sA, a_rs)
                vchain(lambda e, c=c0: e.reciprocal(fv(c + 7, 1), fv(c + 6, 1)))
                v_app = vop(lambda e, x=xt, o=thr, c=c0:
                            e.tensor_scalar(o, x, fv(c + 2, 1), fv(c + 7, 1),
                                            op0=ALU.subtract, op1=ALU.mult))
                last_stats[t % 4] = v_app
                last_store[par] = dma("scalar", dst[t * 128:(t + 1) * 128, :],
                                      thr, SM[f"GL{par}"], waits=[(sV, v_app)])
            barrier()

        def proj_pass(src_tm, wT_hbm, order, evac, pre=None):
            bank_free = [[] for _ in range(4)]
            wsl_user = [[], []]
            xnf_users = []
            for half in range(2):
                r0 = half * HTOK
                plan.waits("sync", xnf_users)
                xnf_users = []
                xr = None
                for c in range(32):
                    xr = dma("sync", bfv(XNF + c * HTOK, HTOK),
                             src_tm[r0:r0 + HTOK, c * 128:(c + 1) * 128],
                             SM["XNF"], transpose=True)
                if pre is not None:
                    pre(half, xr)
                for ei, e in enumerate(order):
                    slot = ei % 2
                    wv = dma("gpsimd",
                             bfv(WSL + slot * 16384, 16384).rearrange(
                                 "p (c e2) -> p c e2", e2=ESL),
                             wT_hbm[:, e * ESL:(e + 1) * ESL].rearrange(
                                 "(c p) e2 -> p c e2", p=128),
                             SM[f"W{slot}"], waits=wsl_user[slot])
                    wsl_user[slot] = []
                    for t in range(HTILES):
                        bk = t % 4
                        bank = PB[bk]
                        plan.waits("tensor", bank_free[bk])
                        bank_free[bk] = []
                        plan.waits("tensor", [wv, xr])
                        close = evac(half, t, e, bank, None, query_only=True)
                        for c in range(32):
                            if c == 31 and close is None:
                                sT.n += 1
                                plan.emit("tensor",
                                          lambda e_, s=sT, c_=c, t_=t, sl=slot, b=bank:
                                          e_.matmul(b[:, 0:512],
                                                    bfv(XNF + c_ * HTOK + t_ * 128, 128),
                                                    bfv(WSL + sl * 16384 + c_ * ESL, ESL),
                                                    start=False, stop=True
                                                    ).then_inc(s.h, 1))
                            else:
                                plan.emit("tensor",
                                          lambda e_, c_=c, t_=t, sl=slot, b=bank,
                                          st=(c == 0):
                                          e_.matmul(b[:, 0:512],
                                                    bfv(XNF + c_ * HTOK + t_ * 128, 128),
                                                    bfv(WSL + sl * 16384 + c_ * ESL, ESL),
                                                    start=st, stop=False))
                        if close is not None:
                            close(bank)
                        te = sT.n
                        bank_free[bk] = evac(half, t, e, bank, te,
                                             query_only=False)
                        wsl_user[slot].append((sT, te))
                        xnf_users.append((sT, te))
            barrier()

        def ffn_phase(res_hbm, dst_hbm):
            state = {}

            def pre(half, xr):
                wsm = dma("gpsimd",
                          bfv(LOC, 4608).rearrange("p (c e2) -> p c e2", e2=144),
                          smallT[:, :].rearrange("(c p) e2 -> p c e2", p=128),
                          SM["SMT"], waits=state.get("smt_user", []))
                tbar = sT.n
                for t in range(HTILES):
                    hbk = PB[4 + t % 2]
                    plan.waits("tensor", state.get(f"hbk{t % 2}", []))
                    plan.waits("tensor", [wsm, xr])
                    sT.n += 1
                    for c in range(32):
                        if c == 31:
                            plan.emit("tensor",
                                      lambda e_, s=sT, c_=c, t_=t, b=hbk:
                                      e_.matmul(b[:, 0:144],
                                                bfv(XNF + c_ * HTOK + t_ * 128, 128),
                                                bfv(LOC + c_ * 144, 144),
                                                start=False, stop=True
                                                ).then_inc(s.h, 1))
                        else:
                            plan.emit("tensor",
                                      lambda e_, c_=c, t_=t, b=hbk, st=(c == 0):
                                      e_.matmul(b[:, 0:144],
                                                bfv(XNF + c_ * HTOK + t_ * 128, 128),
                                                bfv(LOC + c_ * 144, 144),
                                                start=st, stop=False))
                    th = sT.n
                    sc = SMALL + (t % 2) * 16
                    gex = fv(FLOC + (t % 2) * 8, 8)
                    w8 = fv(FLOC + 16 + (t % 2) * 8, 8)
                    plan.wait("vector", sT, th)
                    plan.waits("vector", state.get(f"gat{t % 2}", []))
                    v1 = vop(lambda e, b=hbk, c=sc:
                             e.reduce_max(fv(c, 1), b[:, 128:136], axis=AX,
                                          negate=True))
                    plan.wait("scalar", sV, v1)
                    a1 = aop(lambda e, b=hbk, c=sc, g=gex:
                             e.activation(g, b[:, 128:136], AF.Exp,
                                          bias=fv(c, 1),
                                          accum_out=fv(c + 1, 1)))
                    a2 = aop(lambda e, b=hbk, c=sc:
                             e.activation(fv(c + 2, 1), b[:, 136:137],
                                          AF.Sigmoid, bias=BTHRC))
                    plan.wait("vector", sA, a2)
                    vchain(lambda e, c=sc: e.reciprocal(fv(c + 3, 1),
                                                        fv(c + 1, 1)))
                    vchain(lambda e, c=sc: e.tensor_single_scalar(
                        fv(c + 4, 1), fv(c + 2, 1), 0.125, ALU.mult))
                    vchain(lambda e, c=sc, g=gex, o=w8:
                       e.tensor_scalar(o, g, fv(c + 3, 1), fv(c + 4, 1),
                                       op0=ALU.mult, op1=ALU.subtract))
                    vchain(lambda e, o=w8: e.tensor_single_scalar(
                        o, o, 0.0, ALU.max))
                    vchain(lambda e, c=sc, o=w8:
                           e.reduce_sum(fv(c + 5, 1), o, axis=AX))
                    vchain(lambda e, c=sc: e.tensor_single_scalar(
                        fv(c + 6, 1), fv(c + 5, 1), 0.0, ALU.is_equal))
                    vchain(lambda e, c=sc: e.tensor_add(
                        fv(c + 7, 1), fv(c + 5, 1), fv(c + 6, 1)))
                    vchain(lambda e, c=sc: e.reciprocal(fv(c + 8, 1),
                                                        fv(c + 7, 1)))
                    vchain(lambda e, c=sc: e.tensor_single_scalar(
                        fv(c + 9, 1), fv(c + 8, 1), 2.0, ALU.mult))
                    vchain(lambda e, c=sc, o=w8:
                           e.tensor_scalar(o, o, fv(c + 9, 1), None,
                                           op0=ALU.mult))
                    for ex in range(8):
                        vop(lambda e, b=hbk, ex_=ex, t_=t, o=w8:
                            e.tensor_scalar(
                                bfv(LOC + 5632 + t_ * 128 + ex_ * 16, 16),
                                b[:, ex_ * 16:(ex_ + 1) * 16],
                                o[:, ex_:ex_ + 1], None, op0=ALU.mult))
                    vh = sV.n
                    plan.wait("vector", sV, vh)
                    state[f"hbk{t % 2}"] = [(sV, vh)]
                    state[f"gat{t % 2}"] = [(sV, vh)]
                    plan.wait("tensor", sV, vh)
                    plan.wait("tensor", sT, tbar)
                    plan.waits("tensor", state.get(f"px{t % 2}", []))
                    sT.n += 1
                    plan.emit("tensor", lambda e, s=sT, t_=t:
                              e.transpose(PX[t % 2][:, 0:128],
                                          bfv(LOC + 5632 + t_ * 128, 128),
                                          IDENT[:, :]).then_inc(s.h, 1))
                    plan.wait("vector", sT, sT.n)
                    vht = vop(lambda e, t_=t:
                              e.tensor_copy(bfv(LOC + 6656 + t_ * 128, 128),
                                            PX[t % 2][:, 0:128]))
                    state[f"px{t % 2}"] = [(sV, vht)]
                    state.setdefault("ht", {})[(half, t)] = vht
                state["smt_user"] = [(sT, sT.n)]

            bcs_ready = {}
            bb_ready = {}
            evst = {}

            def evac(half, t, e, bank, te, query_only):
                if query_only:
                    if (half, e) not in bcs_ready:
                        bcs_ready[(half, e)] = dma(
                            "gpsimd", bfv(LOC + 4608 + (e % 2) * 512, 512),
                            bcatT[:, e * ESL:(e + 1) * ESL], SM[f"BCS{e % 2}"],
                            waits=state.get(f"bcsu{e % 2}", []))
                    htv = state["ht"][(half, t)]
                    bw = bcs_ready[(half, e)]

                    def close(bank_, t_=t, e_=e, htv_=htv, bw_=bw):
                        plan.wait("tensor", sV, htv_)
                        plan.waits("tensor", [bw_])
                        sT.n += 1
                        plan.emit("tensor",
                                  lambda en, s=sT, b=bank_, t2=t_, e2=e_:
                                  en.matmul(b[:, 0:512],
                                            bfv(LOC + 6656 + t2 * 128, 128),
                                            bfv(LOC + 4608 + (e2 % 2) * 512, 512),
                                            start=False, stop=True
                                            ).then_inc(s.h, 1))
                        state[f"bcsu{e_ % 2}"] = [(sT, sT.n)]
                    return close
                par = t % 2
                slot = t % 4
                xs = fv(XF + slot * 512, 512)
                rv = dma("sync", xs,
                         res_hbm[half * HTOK + t * 128:
                                 half * HTOK + (t + 1) * 128,
                                 e * ESL:(e + 1) * ESL],
                         SM[f"RES{slot}"], waits=state.get(f"resu{slot}", []))
                if (half, e) not in bb_ready:
                    bb_ready[(half, e)] = dma(
                        "sync", fv(FLOC + 32 + (e % 2) * 512, 512),
                        bbch[:, e * ESL:(e + 1) * ESL], SM[f"BB{e % 2}"],
                        waits=state.get(f"bbu{e % 2}", []))
                plan.waits("gpsimd", [rv, bb_ready[(half, e)]])
                plan.waits("gpsimd", state.get(f"xbu{e % 2}", []))
                xb = fv(FLOC + 1056 + (e % 2) * 512, 512)
                sP.n += 1
                plan.emit("gpsimd",
                          lambda en, s=sP, o=xb, x=xs, e_=e:
                          en.tensor_add(o, x,
                                        fv(FLOC + 32 + (e_ % 2) * 512, 512)
                                        ).then_inc(s.h, 1))
                pv = sP.n
                state[f"bbu{e % 2}"] = [(sP, pv)]
                state[f"resu{slot}"] = [(sP, pv)]
                plan.wait("vector", sP, pv)
                plan.wait("vector", sT, te)
                plan.waits("vector", evst.get(par, []))
                stg = fv(XSTG + par * 512, 512)
                vv = vop(lambda en, b=bank, o=stg, x=xb:
                         en.scalar_tensor_tensor(o, b[:, 0:512], 0.5, x,
                                                 op0=ALU.mult, op1=ALU.add))
                state[f"xbu{e % 2}"] = [(sV, vv)]
                stv = dma("scalar",
                          dst_hbm[half * HTOK + t * 128:
                                  half * HTOK + (t + 1) * 128,
                                  e * ESL:(e + 1) * ESL],
                          stg, SM[f"GX{par}"], waits=[(sV, vv)])
                evst[par] = [stv]
                return [(sV, vv)]

            proj_pass(xnb, wbaseT, list(range(8)), evac, pre=pre)

        def qkv_phase():
            state = {}

            def evac(half, t, e, bank, te, query_only):
                if query_only:
                    return None
                par = (t + e) % 2
                stg = bfv(KSTG + par * 512, 512)
                plan.wait("vector", sT, te)
                plan.waits("vector", state.get(f"stg{par}", []))
                vclip = vop(lambda en, b=bank, o=stg:
                            en.tensor_scalar(o, b[:, 0:512], -8.0, 8.0,
                                             op0=ALU.max, op1=ALU.min))
                frees = [(sV, vclip)]
                if e < 16:
                    is_q = e < 8
                    si = e if is_q else e - 8
                    base = STAT + (256 if is_q else 0) + (half * 8 + t) * 16
                    plan.wait("vector", sV, vclip)
                    vop(lambda en, o=stg, c=base + si:
                        en.reduce_sum(fv(c, 1), o, axis=AX))
                    vsum = sV.n
                    frees.append((sV, vsum))
                    plan.wait("scalar", sV, vclip)
                    plan.waits("scalar", state.get(f"sq{par}", []))
                    asq = aop(lambda en, o=stg, c=base + 8 + si, p_=par:
                              en.activation(bfv(LOC + p_ * 512, 512), o,
                                            AF.Square, accum_out=fv(c, 1)))
                    state[f"sq{par}"] = [(sA, asq)]
                    frees.append((sA, asq))
                if e < 8:
                    dsth, col = qrawb, e * 512
                elif e < 16:
                    dsth, col = krawb, (e - 8) * 512
                else:
                    dsth, col = kvloc, 4096 + (e - 16) * 512
                stv = dma("scalar",
                          dsth[half * HTOK + t * 128:half * HTOK + (t + 1) * 128,
                               col:col + 512], stg, SM[f"GK{par}"],
                          waits=[(sV, vclip)])
                state[f"stg{par}"] = frees + [stv]
                return frees

            order = list(range(8, 16)) + list(range(16, 24)) + list(range(0, 8))
            proj_pass(xnb, wqkvT, order, evac)

        def apply_pass(raw_hbm, dst_hbm, wb_idx, stat_q):
            st = {}
            wv1 = dma("sync", bfv(APLW, 4096),
                      lnwbh[wb_idx * 128:(wb_idx + 1) * 128, :], SM["APLW"])
            wv2 = dma("sync", bfv(APLW + 4096, 4096),
                      lnwbh[(wb_idx + 1) * 128:(wb_idx + 2) * 128, :],
                      SM["APLW"])
            tmpc = SMALL + 100
            for half in range(2):
                for t in range(HTILES):
                    base = STAT + (256 if stat_q else 0) + (half * 8 + t) * 16
                    mc = MURC + (128 if stat_q else 0) + (half * 8 + t) * 2
                    plan.waits("vector", st.get("mu", []))
                    vchain(lambda e, c=base, m=mc:
                           e.reduce_sum(fv(m, 1), fv(c, 8), axis=AX))
                    vchain(lambda e, m=mc: e.tensor_single_scalar(
                        fv(m, 1), fv(m, 1), 1.0 / D, ALU.mult))
                    vchain(lambda e, c=base, tc=tmpc:
                           e.reduce_sum(fv(tc, 1), fv(c + 8, 8), axis=AX))
                    vchain(lambda e, tc=tmpc: e.tensor_single_scalar(
                        fv(tc, 1), fv(tc, 1), 1.0 / D, ALU.mult))
                    vchain(lambda e, m=mc, tc=tmpc: e.tensor_mul(
                        fv(tc + 1, 1), fv(m, 1), fv(m, 1)))
                    v_var = vchain(lambda e, tc=tmpc: e.tensor_sub(
                        fv(tc + 2, 1), fv(tc, 1), fv(tc + 1, 1)))
                    plan.wait("scalar", sV, v_var)
                    ar = aop(lambda e, tc=tmpc: e.activation(
                        fv(tc + 3, 1), fv(tc + 2, 1), AF.Sqrt, bias=EPSC))
                    plan.wait("vector", sA, ar)
                    v_stats = vchain(lambda e, m=mc, tc=tmpc:
                                     e.reciprocal(fv(m + 1, 1), fv(tc + 3, 1)))
                    st["mu"] = [(sV, v_stats)]
                    for piece in range(2):
                        par = piece
                        r0 = half * HTOK + t * 128
                        col = piece * 2048
                        wvp = dma("sync", bfv(APLK + par * 2048, 2048),
                                  raw_hbm[r0:r0 + 128, col:col + 2048],
                                  SM[f"APK{par}"],
                                  waits=st.get(f"apl{par}", []))
                        plan.waits("vector", [wvp, wv2])
                        plan.wait("vector", sV, v_stats)
                        plan.waits("vector", st.get("t1", []))
                        t1 = fv(FLOC, 2048)
                        vchain(lambda e, par_=par, m=mc, o=t1:
                               e.tensor_scalar(o, bfv(APLK + par_ * 2048, 2048),
                                               fv(m, 1), fv(m + 1, 1),
                                               op0=ALU.subtract, op1=ALU.mult))
                        vchain(lambda e, o=t1, col_=col:
                               e.tensor_mul(o, o, bfv(APLW + col_, 2048)))
                        stg = bfv(APLO + par * 2048, 2048)
                        plan.waits("vector", st.get(f"aplo{par}", []))
                        vv = vop(lambda e, i=t1, o=stg, col_=col:
                                 e.tensor_add(o, i, bfv(APLW + 4096 + col_, 2048)))
                        st["t1"] = [(sV, vv)]
                        stv = dma("scalar", dst_hbm[r0:r0 + 128, col:col + 2048],
                                  stg, SM[f"GA{par}"], waits=[(sV, vv)])
                        st[f"apl{par}"] = [stv]
                        st[f"aplo{par}"] = [stv]
            barrier()

        def outproj_phase():
            state = {}
            evst = {}

            def evac(half, t, e, bank, te, query_only):
                if query_only:
                    return None
                par = t % 2
                slot = t % 4
                xs = fv(XF + slot * 512, 512)
                rv = dma("sync", xs,
                         x1b[half * HTOK + t * 128:half * HTOK + (t + 1) * 128,
                             e * ESL:(e + 1) * ESL],
                         SM[f"RES{slot}"], waits=state.get(f"resu{slot}", []))
                plan.waits("vector", [rv])
                plan.wait("vector", sT, te)
                plan.waits("vector", evst.get(par, []))
                stg = fv(XSTG + par * 512, 512)
                vv = vop(lambda en, b=bank, o=stg, x=xs:
                         en.tensor_add(o, b[:, 0:512], x))
                state[f"resu{slot}"] = [(sV, vv)]
                stv = dma("scalar",
                          x2b[half * HTOK + t * 128:half * HTOK + (t + 1) * 128,
                              e * ESL:(e + 1) * ESL],
                          stg, SM[f"GX{par}"], waits=[(sV, vv)])
                evst[par] = [stv]
                return [(sV, vv)]

            proj_pass(xnb, woutT, list(range(8)), evac)

        def attention_phase(cc_done):
            maskv = dma("sync", bfv(AMSK, 4096), maskT_in[:, :], SM["MISC"])
            plan.waits("tensor", [maskv])
            for b in range(2):
                va1 = bfv(AVA + b * 4160, 4160).rearrange(
                    "p (kb ch) -> p kb ch", ch=130)[:, :, 128:130]
                plan.emit("vector", lambda e, a=va1: e.memset(a, 1.0))
            v_vones = vop(lambda e: e.memset(fv(SMALL + 1, 1), 0.0))
            plan.wait("tensor", sV, v_vones)
            kf_user = [[], []]
            va_user = [[], []]
            qf_user = [[], []]
            row_user = [[], []]
            slc_user = [[], []]
            pbuf_user = [[] for _ in range(4)]
            ptbuf_user = [[] for _ in range(4)]
            px_user = [[], []]
            sbank_user = [[] for _ in range(4)]
            pvbank_user = [[], []]
            nb_user = [[], []]
            stg_user = [[], []]
            scal_user = []
            gcnt = [0]
            ccnt = [0]
            tcnt = [0]
            ucnt = [0]

            for h in range(NH):
                hb = h % 2
                gw = [(sCC, cc_done)] if h < 2 else []
                kf = dma("sync", bfv(AKF + hb * 4096, 4096),
                         kvg[:, h * 128:(h + 1) * 128], SM[f"KF{hb}"],
                         waits=kf_user[hb] + gw, transpose=True)
                kf_user[hb] = []
                va = dma("sync",
                         bfv(AVA + hb * 4160, 4160).rearrange(
                             "p (kb ch) -> p kb ch", ch=130)[:, :, 0:128],
                         kvg[:, 4096 + h * 128:4096 + (h + 1) * 128].rearrange(
                             "(kb p) ch -> p kb ch", p=128),
                         SM[f"VA{hb}"], waits=va_user[hb] + gw)
                va_user[hb] = []
                qf = dma("sync", bfv(AQF + hb * 2048, 2048),
                         qnb[:, h * 128:(h + 1) * 128], SM[f"QF{hb}"],
                         waits=qf_user[hb], transpose=True)
                qf_user[hb] = []
                plan.waits("vector", slc_user[0] + slc_user[1])
                slc_user[0] = []
                slc_user[1] = []
                slv = vop(lambda e, v=SLOPES[h]: e.memset(SLBOTH, v))
                plan.waits("vector", stg_user[h % 2])
                stage_done = []
                for p in range(16):
                    rb = p % 2
                    rowv = dma("sync", ALROW[rb], alrows_in[p:p + 1, :],
                               SM[f"ROW{rb}"], waits=row_user[rb])
                    row_user[rb] = []
                    alo = _alo(p, h)
                    L = p + 1 - alo
                    runs = [(alo * 128, L * 128, 0), (2048 + alo * 128, L * 128, 1)]
                    nchunks_run = int(math.ceil(L * 128 / 512.0))
                    ngroups = 1 if 2 * nchunks_run <= 4 else 2
                    groups = [runs] if ngroups == 1 else [[runs[0]], [runs[1]]]
                    gdata = []
                    for gruns in groups:
                        gpar = gcnt[0] % 4
                        gcnt[0] += 1
                        nb_off = SMALL + (gpar % 2) * 16
                        pboff = APB + gpar * 2048
                        ptoff = APT + gpar * 2048
                        chunk_infos = []
                        colpos = 0
                        for (g0, glen, rix) in gruns:
                            off = 0
                            while off < glen:
                                cw = min(512, glen - off)
                                chunk_infos.append(
                                    (g0 + off, cw, colpos, rix,
                                     off + cw == glen))
                                colpos += cw
                                off += cw
                        glen_tot = colpos
                        te_chunks = []
                        for (kcol, cw, pcol, rix, is_end) in chunk_infos:
                            bk = ccnt[0] % 4
                            ccnt[0] += 1
                            bank = PB[bk]
                            plan.waits("tensor", sbank_user[bk])
                            sbank_user[bk] = []
                            plan.waits("tensor", [kf, qf, rowv])
                            plan.wait("tensor", sV, slv)
                            plan.emit("tensor",
                                      lambda e, b=bank, hb_=hb, p_=p, cw_=cw,
                                      kc=kcol:
                                      e.matmul(b[:, 0:cw_],
                                               bfv(AQF + hb_ * 2048 + p_ * 128, 128),
                                               bfv(AKF + hb_ * 4096 + kc, cw_),
                                               start=True, stop=False))
                            sT.n += 1
                            if is_end:
                                plan.emit("tensor",
                                          lambda e, cw_=cw, kc=kcol,
                                          rb_=rb, b=bank:
                                          e.matmul(b[:, 0:cw_], SLCOL[rb_],
                                                   ALROW[rb_][0:1, kc:kc + cw_],
                                                   start=False, stop=False))
                                mcol = (2 * p + rix) * 128
                                plan.emit("tensor",
                                          lambda e, s=sT, b=bank, cw_=cw, mc=mcol:
                                          e.matmul(b[:, cw_ - 128:cw_],
                                                   bfv(AMSK + mc, 128),
                                                   IDENT[:, :],
                                                   start=False, stop=True
                                                   ).then_inc(s.h, 1))
                            else:
                                plan.emit("tensor",
                                          lambda e, s=sT, cw_=cw, kc=kcol,
                                          rb_=rb, b=bank:
                                          e.matmul(b[:, 0:cw_], SLCOL[rb_],
                                                   ALROW[rb_][0:1, kc:kc + cw_],
                                                   start=False, stop=True
                                                   ).then_inc(s.h, 1))
                            te_chunks.append((bk, sT.n, kcol, cw, pcol))
                        for ci, (bk, tev, kcol, cw, pcol) in enumerate(te_chunks):
                            plan.wait("vector", sT, tev)
                            if ci == len(te_chunks) - 1:
                                vop(lambda e, b=PB[bk], c=nb_off + ci, cw_=cw:
                                    e.reduce_max(fv(c, 1), b[:, 0:cw_],
                                                 axis=AX, negate=True))
                            else:
                                plan.emit("vector",
                                          lambda e, b=PB[bk], c=nb_off + ci, cw_=cw:
                                          e.reduce_max(fv(c, 1), b[:, 0:cw_],
                                                       axis=AX, negate=True))
                        plan.wait("vector", sV, sV.n)
                        plan.waits("vector", nb_user[gpar % 2])
                        nb_user[gpar % 2] = []
                        vbias = vop(lambda e, c=nb_off, n=len(te_chunks):
                                    e.tensor_reduce(fv(c + 8, 1), fv(c, n), AX,
                                                    ALU.min))
                        plan.wait("scalar", sV, vbias)
                        plan.waits("scalar", pbuf_user[gpar])
                        pbuf_user[gpar] = []
                        ae_last = 0
                        for (bk, tev, kcol, cw, pcol) in te_chunks:
                            ae_last = aop(lambda e, b=PB[bk], c=nb_off,
                                          po=pboff + pcol, cw_=cw:
                                          e.activation(bfv(po, cw_), b[:, 0:cw_],
                                                       AF.Exp,
                                                       bias=fv(c + 8, 1)))
                            sbank_user[bk] = [(sA, ae_last)]
                        nb_user[gpar % 2] = [(sA, ae_last)]
                        nblk = glen_tot // 128
                        vt_last = 0
                        boff = 0
                        while boff < nblk:
                            nb_b = min(8, nblk - boff)
                            par_t = tcnt[0] % 2
                            tcnt[0] += 1
                            plan.wait("tensor", sA, ae_last)
                            plan.waits("tensor", px_user[par_t])
                            px_user[par_t] = []
                            sT.n += 1
                            for j in range(nb_b):
                                if j == nb_b - 1:
                                    plan.emit("tensor",
                                              lambda e, s=sT, j_=j, bo=boff,
                                              po=pboff, pt=par_t:
                                              e.transpose(
                                                  PX[pt][:, j_ * 128:(j_ + 1) * 128],
                                                  bfv(po + (bo + j_) * 128, 128),
                                                  IDENT[:, :]).then_inc(s.h, 1))
                                else:
                                    plan.emit("tensor",
                                              lambda e, j_=j, bo=boff,
                                              po=pboff, pt=par_t:
                                              e.transpose(
                                                  PX[pt][:, j_ * 128:(j_ + 1) * 128],
                                                  bfv(po + (bo + j_) * 128, 128),
                                                  IDENT[:, :]))
                            ttv = sT.n
                            plan.wait("vector", sT, ttv)
                            plan.waits("vector", ptbuf_user[gpar])
                            ptbuf_user[gpar] = []
                            vt_last = vop(lambda e, n=nb_b, pt=par_t,
                                          po=ptoff + boff * 128:
                                          e.tensor_copy(bfv(po, n * 128),
                                                        PX[pt][:, 0:n * 128]))
                            px_user[par_t] = [(sV, vt_last)]
                            boff += nb_b
                        pbuf_user[gpar] = [(sT, sT.n)]
                        upar = ucnt[0] % 2
                        ucnt[0] += 1
                        pvb = PB[4 + upar]
                        plan.waits("tensor", pvbank_user[upar])
                        pvbank_user[upar] = []
                        plan.wait("tensor", sV, vt_last)
                        plan.waits("tensor", [va])
                        bix = 0
                        sT.n += 1
                        for (g0, glen, rix) in gruns:
                            for bl in range(glen // 128):
                                kb = (g0 // 128) + bl
                                if bix == nblk - 1:
                                    plan.emit("tensor",
                                              lambda e, s=sT, b=pvb, pt=ptoff,
                                              bi=bix, hb_=hb, kb_=kb,
                                              st=(bix == 0):
                                              e.matmul(b[:, 0:129],
                                                       bfv(pt + bi * 128, 128),
                                                       bfv(AVA + hb_ * 4160
                                                           + kb_ * 130, 129),
                                                       start=st, stop=True
                                                       ).then_inc(s.h, 1))
                                else:
                                    plan.emit("tensor",
                                              lambda e, b=pvb, pt=ptoff,
                                              bi=bix, hb_=hb, kb_=kb,
                                              st=(bix == 0):
                                              e.matmul(b[:, 0:129],
                                                       bfv(pt + bi * 128, 128),
                                                       bfv(AVA + hb_ * 4160
                                                           + kb_ * 130, 129),
                                                       start=st, stop=False))
                                bix += 1
                        tvv = sT.n
                        ptbuf_user[gpar] = [(sT, tvv)]
                        gdata.append(dict(gpar=gpar, upar=upar, tv=tvv,
                                          nb=nb_off, pvb=pvb))
                    sc = SMALL + 64
                    stgv = bfv(ASTG + (h % 2) * 2048 + p * 128, 128)
                    if len(gdata) == 1:
                        g0d = gdata[0]
                        plan.wait("vector", sT, g0d["tv"])
                        vchain(lambda e, b=g0d["pvb"], c=sc:
                               e.reciprocal(fv(c, 1), b[:, 128:129]))
                        vfin = vop(lambda e, b=g0d["pvb"], c=sc, o=stgv:
                                   e.tensor_scalar(o, b[:, 0:128], fv(c, 1),
                                                   None, op0=ALU.mult))
                        pvbank_user[g0d["upar"]].append((sV, vfin))
                    else:
                        gA, gB = gdata
                        plan.wait("vector", sT, gB["tv"])
                        plan.wait("vector", sT, gA["tv"])
                        plan.waits("vector", scal_user)
                        vchain(lambda e, ca=gA["nb"] + 8, cb=gB["nb"] + 8, c=sc:
                               e.tensor_tensor(fv(c, 1), fv(ca, 1), fv(cb, 1),
                                               ALU.min))
                        vchain(lambda e, ca=gA["nb"] + 8, c=sc:
                               e.tensor_sub(fv(c + 1, 1), fv(c, 1), fv(ca, 1)))
                        v_d = vchain(lambda e, cb=gB["nb"] + 8, c=sc:
                                     e.tensor_sub(fv(c + 2, 1), fv(c, 1),
                                                  fv(cb, 1)))
                        plan.wait("scalar", sV, v_d)
                        aop(lambda e, c=sc:
                            e.activation(fv(c + 3, 1), fv(c + 1, 1), AF.Exp))
                        a_s = aop(lambda e, c=sc:
                                  e.activation(fv(c + 4, 1), fv(c + 2, 1),
                                               AF.Exp))
                        plan.wait("vector", sA, a_s)
                        cmb = fv(sc + 16, 129)
                        vchain(lambda e, b=gA["pvb"], c=sc, o=cmb:
                               e.tensor_scalar(o, b[:, 0:129], fv(c + 3, 1),
                                               None, op0=ALU.mult))
                        pvbank_user[gA["upar"]].append((sV, sV.n))
                        vchain(lambda e, b=gB["pvb"], c=sc, o=cmb:
                               e.scalar_tensor_tensor(o, b[:, 0:129],
                                                      fv(c + 4, 1), o,
                                                      op0=ALU.mult,
                                                      op1=ALU.add))
                        pvbank_user[gB["upar"]].append((sV, sV.n))
                        vchain(lambda e, c=sc, o=cmb:
                               e.reciprocal(fv(c + 5, 1), o[:, 128:129]))
                        vfin = vop(lambda e, c=sc, o=cmb, og=stgv:
                                   e.tensor_scalar(og, o[:, 0:128], fv(c + 5, 1),
                                                   None, op0=ALU.mult))
                        scal_user = [(sV, vfin)]
                    stage_done.append((sV, vfin))
                    lastt = gdata[-1]["tv"]
                    row_user[rb].append((sT, lastt))
                    kf_user[hb].append((sT, lastt))
                    va_user[hb].append((sT, lastt))
                    qf_user[hb].append((sT, lastt))
                    slc_user[hb].append((sT, lastt))
                stv = dma("scalar",
                          xnb[:, h * 128:(h + 1) * 128].rearrange(
                              "(qt p) ch -> p qt ch", p=128),
                          bfv(ASTG + (h % 2) * 2048, 2048).rearrange(
                              "p (qt ch) -> p qt ch", ch=128),
                          SM[f"GS{h % 2}"], waits=stage_done)
                stg_user[h % 2] = [stv]
            barrier()

        ln_pass(x_in, xnb)
        ffn_phase(x_in, x1b)

        if KSTOP == "ffn1":
            for j in range(4):
                dma("sync", out[j * 512:(j + 1) * 512, :],
                    x1b[j * 512:(j + 1) * 512, :], SM["DBG"])
        else:
            ln_pass(x1b, xnb)
            qkv_phase()
            apply_pass(krawb, kvloc, 2, stat_q=False)
            plan.waits("gpsimd", [(SM["GA0"], SM["GA0"].n),
                                  (SM["GA1"], SM["GA1"].n),
                                  (SM["GK0"], SM["GK0"].n),
                                  (SM["GK1"], SM["GK1"].n)])
            sCC.n += 1
            plan.emit("gpsimd", lambda e, s=sCC:
                      e.collective_compute(
                          "AllGather", ALU.bypass,
                          replica_groups=[[0, 1], [2, 3], [4, 5], [6, 7]],
                          ins=[kvloc[:, :]],
                          outs=[kvg[:, :]],
                      ).then_inc(s.h, 1))
            cc_done = sCC.n
            apply_pass(qrawb, qnb, 0, stat_q=True)

            if KSTOP == "qraw":
                for j in range(4):
                    dma("gpsimd", out[j * 512:(j + 1) * 512, :],
                        qrawb[j * 512:(j + 1) * 512, :], SM["DBG"])
                plan.wait("gpsimd", sCC, cc_done)
            elif KSTOP == "qn":
                for j in range(4):
                    dma("gpsimd", out[j * 512:(j + 1) * 512, :],
                        qnb[j * 512:(j + 1) * 512, :], SM["DBG"])
                plan.wait("gpsimd", sCC, cc_done)
            elif KSTOP == "kvk":
                plan.wait("gpsimd", sCC, cc_done)
                for j in range(4):
                    dma("gpsimd", out[j * 512:(j + 1) * 512, :],
                        kvg[j * 512:(j + 1) * 512, 0:4096], SM["DBG"])
            else:
                attention_phase(cc_done)
                outproj_phase()
                if KSTOP == "attn":
                    for j in range(4):
                        dma("sync", out[j * 512:(j + 1) * 512, :],
                            x2b[j * 512:(j + 1) * 512, :], SM["DBG"])
                elif KSTOP == "fullx2":
                    ln_pass(x2b, xnb)
                    ffn_phase(x2b, x1b)
                    for j in range(4):
                        dma("gpsimd", out[j * 512:(j + 1) * 512, :],
                            x2b[j * 512:(j + 1) * 512, :], SM["DBG"])
                elif KSTOP == "ffn2d":
                    ln_pass(x2b, xnb)
                    ffn_phase(x2b, x1b)
                    for j in range(4):
                        dma("gpsimd", out[j * 512:(j + 1) * 512, :],
                            x1b[j * 512:(j + 1) * 512, :], SM["DBG"])
                elif KSTOP == "ln3":
                    ln_pass(x2b, xnb)
                    for j in range(4):
                        dma("gpsimd", out[j * 512:(j + 1) * 512, :],
                            xnb[j * 512:(j + 1) * 512, :], SM["DBG"])
                else:
                    ln_pass(x2b, xnb)
                    ffn_phase(x2b, out)

        barrier()

        for s in plan.sems:
            assert s.n < 60000, (s.name, s.n)
        if os.environ.get("KPRINT"):
            print("sem totals:", {s.name: s.n for s in plan.sems if s.n})

        with nc.Block() as block:
            @block.sync
            def _(e):
                for fn in plan.q["sync"]:
                    fn(e)

            @block.scalar
            def _(e):
                for fn in plan.q["scalar"]:
                    fn(e)

            @block.vector
            def _(e):
                for fn in plan.q["vector"]:
                    fn(e)

            @block.tensor
            def _(e):
                for fn in plan.q["tensor"]:
                    fn(e)

            @block.gpsimd
            def _(e):
                for fn in plan.q["gpsimd"]:
                    fn(e)

    n_instr = sum(len(v) for v in plan.q.values())
    return nc, n_instr


def _host_constants():
    ident = np.eye(128, dtype=np.float32).astype(ml_dtypes.bfloat16)
    gpos = np.zeros(S, dtype=np.int64)
    for i, g in enumerate(BLK_A):
        gpos[i * 128:(i + 1) * 128] = g * 128 + np.arange(128)
    for i, g in enumerate(BLK_B):
        gpos[2048 + i * 128:2048 + (i + 1) * 128] = g * 128 + np.arange(128)

    maskTs, alrows = [], []
    for role, own in ((0, BLK_A), (1, BLK_B)):
        mt = np.zeros((128, 4096), dtype=np.float32)
        for p in range(16):
            q0 = own[p] * 128
            for j in range(2):
                gk = BLK_A[p] if j == 0 else BLK_B[p]
                k0 = gk * 128
                qpos = q0 + np.arange(128)[:, None]
                kpos = k0 + np.arange(128)[None, :]
                M = np.where(kpos <= qpos, 0.0, -1.0e30).astype(np.float32)
                mt[:, (2 * p + j) * 128:(2 * p + j + 1) * 128] = M.T
        maskTs.append(mt.astype(ml_dtypes.bfloat16))
        ar = np.zeros((16, S), dtype=np.float32)
        for p in range(16):
            qend = own[p] * 128 + 127
            ar[p] = (gpos - qend).astype(np.float32)
        alrows.append(ar.astype(ml_dtypes.bfloat16))
    return ident, maskTs, alrows


def kernel(L0_attn_hidden_states, L0_attn_w_qkv, L0_attn_q_ln_w, L0_attn_q_ln_b,
           L0_attn_k_ln_w, L0_attn_k_ln_b, L0_attn_w_out, L0_ffn_W_base,
           L0_ffn_b_base, L0_ffn_W_gate, L0_ffn_W_thr, L0_ffn_b_thr,
           L0_ffn_A_w, L0_ffn_B_w):
    x = np.asarray(L0_attn_hidden_states, dtype=np.float32)
    Bsz = x.shape[0]

    wqkvT = np.ascontiguousarray(np.asarray(L0_attn_w_qkv, np.float32).T)
    woutT = np.ascontiguousarray(np.asarray(L0_attn_w_out, np.float32).T)
    wbaseT = np.ascontiguousarray(np.asarray(L0_ffn_W_base, np.float32).T)
    A_w = np.asarray(L0_ffn_A_w, np.float32)
    B_w = np.asarray(L0_ffn_B_w, np.float32)
    smallT = np.zeros((D, 144), dtype=np.float32)
    smallT[:, 0:128] = A_w.reshape(128, D).T
    smallT[:, 128:136] = np.asarray(L0_ffn_W_gate, np.float32).T
    smallT[:, 136] = np.asarray(L0_ffn_W_thr, np.float32)
    bcatT = np.ascontiguousarray(np.transpose(B_w, (0, 2, 1)).reshape(128, D))
    rows = np.zeros((6, D), dtype=np.float32)
    rows[0] = np.asarray(L0_attn_q_ln_w, np.float32)
    rows[1] = np.asarray(L0_attn_q_ln_b, np.float32)
    rows[2] = np.asarray(L0_attn_k_ln_w, np.float32)
    rows[3] = np.asarray(L0_attn_k_ln_b, np.float32)
    rows[4] = np.asarray(L0_ffn_b_base, np.float32)
    rows[5, 0] = np.asarray(L0_ffn_b_thr, np.float32).reshape(-1)[0]

    ident, maskTs, alrows = _host_constants()

    nc, n_instr = build_program()

    in_maps = []
    shard_rows = []
    for core in range(8):
        b = core // 2
        role = core % 2
        own = BLK_A if role == 0 else BLK_B
        ridx = np.concatenate([np.arange(g * 128, (g + 1) * 128) for g in own])
        shard_rows.append(ridx)
        xs = np.ascontiguousarray(x[b][ridx])
        in_maps.append(dict(
            x=xs, wqkvT=wqkvT, woutT=woutT, wbaseT=wbaseT, smallT=smallT,
            bcatT=bcatT, rows=rows, maskT=maskTs[role], alrows=alrows[role],
            ident=ident))

    res = run_bass_kernel_spmd(nc, in_maps, core_ids=list(range(8)))

    outa = np.zeros((Bsz, S, D), dtype=np.float32)
    for core in range(8):
        b = core // 2
        outa[b][shard_rows[core]] = res.results[core]["out"]
    return outa
